# version 18
# baseline (speedup 1.0000x reference)
import numpy as np
import ml_dtypes
import concourse.bass as bass
import concourse.mybir as mybir
from concourse import dve_ops
from concourse.bass_utils import run_bass_kernel_spmd
from contextlib import ExitStack

F32 = mybir.dt.float32
BF16 = mybir.dt.bfloat16
I8 = mybir.dt.int8
AF = mybir.ActivationFunctionType
ALU = mybir.AluOpType
AX = mybir.AxisListType.X

NCORES = 8
T = 1024
D = 4096
NT = 8
NCH = 32
NH = 8
QG = (3, 3, 2)


def _tile_lhsT(W):
    O, I = W.shape
    t = np.ascontiguousarray(W.T).reshape(I // 128, 128, O // 128, 128)
    return np.ascontiguousarray(t.transpose(2, 1, 0, 3))


def build_nc():
    nc = bass.Bass("TRN2", target_bir_lowering=False)
    dr = lambda n, sh, dt, k="ExternalInput": nc.dram_tensor(n, sh, dt, kind=k).ap()
    x_rm = dr("x_rm", [T, D], F32)
    x_ft = dr("x_ft", [D, T], F32)
    wq_t = dr("wq_t", [NH, 128, NCH, 128], F32)
    wo_t = dr("wo_t", [NCH, 128, NCH, 128], F32)
    wf_t = dr("wf_t", [NCH, 128, NCH, 128], I8)
    m_t = dr("m_t", [128, NH * 128], F32)
    zT = dr("zT", [128, NH], F32)
    bq_c = dr("bq_c", [128, NH], F32)
    ws_c = dr("ws_c", [128, NCH], F32)
    bi_c = dr("bi_c", [128, NCH], F32)
    consts = dr("consts", [128, 4], F32)
    ident_b = dr("ident_b", [128, 128], BF16)
    ident_f = dr("ident_f", [128, 128], F32)
    ones_rf = dr("ones_rf", [1, 128], F32)
    ones_cb = dr("ones_cb", [128, 1], BF16)
    out_T = dr("out_T", [D, T], F32, "ExternalOutput")

    st = ExitStack()
    with st:
        sb = lambda n, sh, dt=F32: st.enter_context(nc.sbuf_tensor(n, sh, dt))
        xq_T = sb("xq_T", [128, NCH * T], BF16)
        cols = sb("cols", [128, 24])
        acols = sb("acols", [128, 16])
        mcols = sb("mcols", [128, 8])
        tcol = sb("tcol", [128, 8])
        sdcol = sb("sdcol", [128, 8])
        varp = sb("varp", [128, 8])
        invc = sb("invc", [128, 8])
        sm_t = sb("sm_t", [128, NH * 128], BF16)
        szT = sb("szT", [128, NH], BF16)
        sbq = sb("sbq", [128, NH])
        sbqn = sb("sbqn", [128, NH])
        sws = sb("sws", [128, NCH])
        sbi = sb("sbi", [128, NCH])
        scst = sb("scst", [128, 4])
        sidb = sb("sidb", [128, 128], BF16)
        sidf = sb("sidf", [128, 128], F32)
        sonrf = sb("sonrf", [1, 128], F32)
        soncb = sb("soncb", [128, 1], BF16)
        r_b = sb("r_b", [128, T])
        xs_b = sb("xs_b", [128, T])
        negm_row = sb("negm_row", [1, T], BF16)

        sx = st.enter_context(nc.semaphore("sx"))
        sg = st.enter_context(nc.semaphore("sg"))
        sv = st.enter_context(nc.semaphore("sv"))
        ss = st.enter_context(nc.semaphore("ss"))
        stt = st.enter_context(nc.semaphore("stt"))
        block = st.enter_context(nc.Block())

        C = {"x": 0, "g": 0, "v": 0, "s": 0, "t": 0}
        SEM = {"x": sx, "g": sg, "v": sv, "s": ss, "t": stt}

        def mk(tag, dma=False):
            d = 16 if dma else 1
            def f(inst):
                inst.then_inc(SEM[tag], d)
                C[tag] += d
                return C[tag]
            return f
        vop, sop, top = mk("v"), mk("s"), mk("t")
        xdma, gdma = mk("x", True), mk("g", True)

        class Stream:
            def __init__(self, name, n=2):
                self.sems = [st.enter_context(nc.semaphore(f"{name}{i}"))
                             for i in range(n)]
                self.cnt = [0] * n
                self.n = n
            def issue(self, inst, i):
                p = i % self.n
                inst.then_inc(self.sems[p], 16)
                self.cnt[p] += 16
                return (p, self.cnt[p])
            def wait(self, eng, tok):
                if tok:
                    eng.wait_ge(self.sems[tok[0]], tok[1])
            def wait_all(self, eng):
                for p in range(self.n):
                    if self.cnt[p]:
                        eng.wait_ge(self.sems[p], self.cnt[p])

        stx = Stream("stx")
        stxf = Stream("stxf", 3)
        stwq = Stream("stwq", 3)
        stwf = Stream("stwf", 3)
        stwo = Stream("stwo")
        stxr = Stream("stxr")
        stout = Stream("stout")
        ALL_STREAMS = [stx, stxf, stwq, stwf, stwo, stxr, stout]

        def full_barrier():
            marks = dict(C)
            for name in ("sync", "gpsimd", "vector", "scalar", "tensor"):
                sec = getattr(block, name)
                @sec
                def _(eng, marks=marks):
                    for tag in ("x", "g", "v", "s", "t"):
                        if marks[tag] > 0:
                            eng.wait_ge(SEM[tag], marks[tag])
                    for stream in ALL_STREAMS:
                        stream.wait_all(eng)

        @block.sync
        def _(sy):
            xdma(sy.dma_start(out=sbq[:, :], in_=bq_c))
            xdma(sy.dma_start(out=sws[:, :], in_=ws_c))
            xdma(sy.dma_start(out=sbi[:, :], in_=bi_c))
            xdma(sy.dma_start(out=scst[:, :], in_=consts))
            xdma(sy.dma_start(out=sidb[:, :], in_=ident_b))
            xdma(sy.dma_start(out=sidf[:, :], in_=ident_f))
            xdma(sy.dma_start(out=sonrf[:, :], in_=ones_rf))
            xdma(sy.dma_start(out=soncb[:, :], in_=ones_cb))
        p0_x = C["x"]
        @block.gpsimd
        def _(g):
            gdma(g.dma_start(out=sm_t[:, :], in_=m_t))
            gdma(g.dma_start(out=szT[:, :], in_=zT))
        @block.scalar
        def _(sc):
            sc.preload_act_table(AF.Sqrt)
        @block.vector
        def _(v):
            v.wait_ge(sx, p0_x)
            vop(v.tensor_scalar(sbqn[:, :], sbq[:, :], -1.0, None, ALU.mult))

        neg1_col = scst[:, 1:2]
        neg257_col = scst[:, 2:3]

        with (
            nc.sbuf_tensor("bx", [128, 2, D], F32) as bx,
            nc.sbuf_tensor("bxn", [128, D], F32) as bxn,
            nc.sbuf_tensor("scr", [128, D], F32) as scr,
            nc.sbuf_tensor("q0i", [128, D], mybir.dt.int32) as q0i,
            nc.sbuf_tensor("fm1", [128, D], F32) as fm1,
            nc.sbuf_tensor("bqbf", [128, 2, D], BF16) as bqbf,
            nc.sbuf_tensor("dums", [128, D], BF16) as dums,
            nc.sbuf_tensor("tp24", [24, 128], F32) as tp24,
            nc.sbuf_tensor("r_row", [1, T], F32) as r_row,
            nc.sbuf_tensor("xs_row", [1, T], F32) as xs_row,
            nc.psum_tensor("ptr", [128, 2, 1024], BF16) as ptr,
            nc.psum_tensor("pc24", [24, 128], F32) as pc24,
            nc.psum_tensor("pbc", [128, 1024], F32) as pbc,
        ):
            ld_done = [None] * NT
            sq_done = [0] * NT
            vcol_done = [0] * NT
            sqrt_done = [0] * NT
            xn_done = [0] * NT
            q0_done = [0] * NT
            fm1_done = [0] * NT
            e1_done = [0] * NT
            xq_done = [0] * NT
            trg = [[0] * 4 for _ in range(NT)]
            evg = [[0] * 4 for _ in range(NT)]

            def U1(tau):
                @block.sync
                def _(sy, tau=tau):
                    if tau >= 2:
                        sy.wait_ge(ss, sq_done[tau - 2])
                        sy.wait_ge(sv, xn_done[tau - 2])
                    ld_done[tau] = stx.issue(sy.dma_start(
                        out=bx[:, tau % 2, :],
                        in_=x_rm[128 * tau:128 * (tau + 1), :]), tau)

            def U2(tau):
                @block.scalar
                def _(sc, tau=tau):
                    stx.wait(sc, ld_done[tau])
                    sc.wait_ge(ss, C["s"])
                    sop(sc.activation(dums[:, :], bx[:, tau % 2, :], AF.Square,
                                      accum_out=acols[:, 2 * tau + 1:2 * tau + 2]))
                    sc.wait_ge(ss, C["s"])
                    sq_done[tau] = sop(sc.activation(
                        dums[:, :], bx[:, tau % 2, :], AF.Copy,
                        accum_out=acols[:, 2 * tau:2 * tau + 1]))

            def U3(tau):
                @block.vector
                def _(v, tau=tau):
                    a_sum = acols[:, 2 * tau:2 * tau + 1]
                    a_sq = acols[:, 2 * tau + 1:2 * tau + 2]
                    mc = mcols[:, tau:tau + 1]
                    v.wait_ge(ss, sq_done[tau])
                    vop(v.tensor_scalar(mc, a_sum, 1.0 / D, None, ALU.mult))
                    vop(v.tensor_scalar(tcol[:, 0:1], a_sq, 1.0 / D, None,
                                        ALU.mult))
                    v.wait_ge(sv, C["v"])
                    vop(v.tensor_tensor(tcol[:, 1:2], mc, mc, ALU.mult))
                    v.wait_ge(sv, C["v"])
                    vop(v.tensor_tensor(tcol[:, 2:3], tcol[:, 0:1], tcol[:, 1:2],
                                        ALU.subtract))
                    v.wait_ge(sv, C["v"])
                    vcol_done[tau] = vop(v.tensor_scalar(
                        varp[:, tau:tau + 1], tcol[:, 2:3], 1e-5, None, ALU.add))
                    vop(v.tensor_scalar(cols[:, tau:tau + 1], mc, -1.0, None,
                                        ALU.mult))

            def U4(tau):
                @block.scalar
                def _(sc, tau=tau):
                    sc.wait_ge(sv, vcol_done[tau])
                    sqrt_done[tau] = sop(sc.activation(
                        sdcol[:, tau:tau + 1], varp[:, tau:tau + 1], AF.Sqrt))

            def U5(tau):
                @block.vector
                def _(v, tau=tau):
                    rc = cols[:, 8 + tau:9 + tau]
                    vp = varp[:, tau:tau + 1]
                    v.wait_ge(ss, sqrt_done[tau])
                    vop(v.reciprocal(tcol[:, 3:4], sdcol[:, tau:tau + 1]))
                    v.wait_ge(sv, C["v"])
                    vop(v.tensor_tensor(tcol[:, 4:5], tcol[:, 3:4], tcol[:, 3:4],
                                        ALU.mult))
                    v.wait_ge(sv, C["v"])
                    vop(v.tensor_tensor(tcol[:, 5:6], tcol[:, 4:5], vp, ALU.mult))
                    v.wait_ge(sv, C["v"])
                    vop(v.tensor_scalar(tcol[:, 6:7], tcol[:, 5:6], -0.5, 1.5,
                                        ALU.mult, ALU.add))
                    v.wait_ge(sv, C["v"])
                    vop(v.tensor_tensor(rc, tcol[:, 3:4], tcol[:, 6:7], ALU.mult))
                    v.wait_ge(sv, C["v"])
                    xn_done[tau] = vop(v.tensor_scalar(
                        bxn[:, :], bx[:, tau % 2, :], mcols[:, tau:tau + 1], rc,
                        ALU.subtract, ALU.mult))
                    v.wait_ge(sv, C["v"])
                    vop(v.reduce_max(tcol[:, 7:8], bxn[:, :], axis=AX,
                                     apply_absolute_value=True))
                    v.wait_ge(sv, C["v"])
                    vop(v.tensor_scalar(cols[:, 16 + tau:17 + tau], tcol[:, 7:8],
                                        1e-8, float(1.0 / 127.0), ALU.max,
                                        ALU.mult))
                    v.wait_ge(sv, C["v"])
                    vop(v.reciprocal(invc[:, tau:tau + 1],
                                     cols[:, 16 + tau:17 + tau]))
                    v.wait_ge(sv, C["v"])
                    vop(v.tensor_scalar(scr[:, :], bxn[:, :],
                                        invc[:, tau:tau + 1], 256.5,
                                        ALU.mult, ALU.add))
                    v.wait_ge(sv, C["v"])
                    if tau >= 1:
                        v.wait_ge(ss, fm1_done[tau - 1])
                    q0_done[tau] = vop(v.tensor_copy(q0i[:, :], scr[:, :]))

            def U6(tau):
                @block.scalar
                def _(sc, tau=tau):
                    sc.wait_ge(sv, q0_done[tau])
                    if tau >= 1:
                        sc.wait_ge(sv, e1_done[tau - 1])
                    fm1_done[tau] = sop(sc.activation(fm1[:, :], q0i[:, :],
                                                      AF.Identity,
                                                      bias=neg257_col))

            def U7(tau):
                @block.vector
                def _(v, tau=tau):
                    v.wait_ge(ss, fm1_done[tau])
                    vop(v.scalar_tensor_tensor(
                        bxn[:, :], scr[:, :], -257.0, fm1[:, :], ALU.add,
                        ALU.subtract))
                    v.wait_ge(sv, C["v"])
                    e1_done[tau] = vop(v.scalar_tensor_tensor(
                        scr[:, :], bxn[:, :], 0.0, fm1[:, :], ALU.is_ge,
                        ALU.add))
                    v.wait_ge(sv, C["v"])
                    if tau >= 2:
                        v.wait_ge(stt, trg[tau - 2][3])
                    xq_done[tau] = vop(v.scalar_tensor_tensor(
                        bqbf[:, tau % 2, :], bxn[:, :], 1.0, scr[:, :],
                        ALU.is_ge, ALU.add))

            xqv = xq_T[:, :].rearrange("p (c t1) -> p c t1", t1=T)

            def U8(tau):
                for grp in range(4):
                    @block.tensor
                    def _(t, tau=tau, grp=grp):
                        if tau == 0 and grp == 0:
                            t.wait_ge(sx, p0_x)
                        t.wait_ge(sv, xq_done[tau])
                        prev = (evg[tau][grp - 2] if grp >= 2 else
                                (evg[tau - 1][grp + 2] if tau >= 1 else 0))
                        if prev:
                            t.wait_ge(ss, prev)
                        for j in range(8):
                            c = 8 * grp + j
                            top(t.transpose(
                                ptr[:, grp % 2, 128 * j:128 * (j + 1)],
                                bqbf[:, tau % 2, 128 * c:128 * (c + 1)],
                                sidb[:, :]))
                        trg[tau][grp] = C["t"]
                    @block.scalar
                    def _(sc, tau=tau, grp=grp):
                        sc.wait_ge(stt, trg[tau][grp])
                        evg[tau][grp] = sop(sc.copy(
                            xqv[:, 8 * grp:8 * grp + 8,
                                128 * tau:128 * (tau + 1)],
                            ptr[:, grp % 2, :].rearrange("p (j t1) -> p j t1",
                                                         t1=128)))

            U1(0); U2(0); U3(0); U4(0)
            U1(1); U2(1)
            U5(0); U6(0)
            U3(1); U4(1)
            U1(2); U2(2)
            for tau in range(NT):
                U7(tau)
                U8(tau)
                if tau + 1 < NT:
                    U5(tau + 1); U6(tau + 1)
                if tau + 2 < NT:
                    U3(tau + 2); U4(tau + 2)
                if tau + 3 < NT:
                    U1(tau + 3); U2(tau + 3)

            @block.tensor
            def _(t):
                t.wait_ge(sv, C["v"])
                top(t.transpose(pc24[:, :], cols[:, :], sidf[:, :]))
            @block.scalar
            def _(sc):
                sc.wait_ge(stt, C["t"])
                sop(sc.copy(tp24[:, :], pc24[:, :]))
            @block.gpsimd
            def _(g):
                g.wait_ge(ss, C["s"])
                gdma(g.dma_start(
                    out=negm_row[:, :].rearrange("o (b t1) -> o b t1", b=8),
                    in_=tp24[0:8, :]))
                gdma(g.dma_start(
                    out=r_row[:, :].rearrange("o (b t1) -> o b t1", b=8),
                    in_=tp24[8:16, :]))
                gdma(g.dma_start(
                    out=xs_row[:, :].rearrange("o (b t1) -> o b t1", b=8),
                    in_=tp24[16:24, :]))
            @block.tensor
            def _(t):
                t.wait_ge(sg, C["g"])
                top(t.matmul(pbc[:, 0:512], sonrf[:, :], r_row[:, 0:512],
                             start=True, stop=True))
                top(t.matmul(pbc[:, 512:1024], sonrf[:, :], r_row[:, 512:1024],
                             start=True, stop=True))
            @block.scalar
            def _(sc):
                sc.wait_ge(stt, C["t"])
                sop(sc.copy(r_b[:, :], pbc[:, :]))
            @block.tensor
            def _(t):
                t.wait_ge(ss, C["s"])
                top(t.matmul(pbc[:, 0:512], sonrf[:, :], xs_row[:, 0:512],
                             start=True, stop=True))
                top(t.matmul(pbc[:, 512:1024], sonrf[:, :], xs_row[:, 512:1024],
                             start=True, stop=True))
            @block.scalar
            def _(sc):
                sc.wait_ge(stt, C["t"])
                sop(sc.copy(xs_b[:, :], pbc[:, :]))

            full_barrier()

        sigT = st.enter_context(nc.sbuf_tensor("sigT", [128, NH * T], BF16))
        es_b = ExitStack()
        wqs = es_b.enter_context(nc.sbuf_tensor("wqs", [128, 3, D], BF16))
        qtmp = es_b.enter_context(nc.sbuf_tensor("qtmp", [128, T], F32))
        er0 = es_b.enter_context(nc.sbuf_tensor("er0", [128, T], F32))
        ern = es_b.enter_context(nc.sbuf_tensor("ern", [128, T], F32))
        wrowb = es_b.enter_context(nc.sbuf_tensor("wrowb", [1, 2, 128], BF16))
        es_bp = ExitStack()
        pq = es_bp.enter_context(nc.psum_tensor("pq", [128, 3, T], F32))
        pwr1 = es_bp.enter_context(nc.psum_tensor("pwr1", [1, 2, 128], F32))
        es_xt = ExitStack()
        xtf = es_xt.enter_context(nc.sbuf_tensor("xtf", [128, 3, T], F32))
        xT_bf = es_xt.enter_context(nc.sbuf_tensor("xT_bf", [128, NCH, T], BF16))

        xf_ld = [None] * NCH
        cast_done = [0] * NCH
        wq_ld = [None] * NH
        stripw_done = [0] * NH
        rank1_done = [0] * NH
        wrow_done = [0] * NH
        qtmp_done = [0] * NH
        ee_done = [0] * NH
        sig_done = [0] * NH

        @block.scalar
        def _(sc):
            sc.preload_act_table(AF.Exp)

        for k in range(NCH):
            @block.sync
            def _(sy, k=k):
                if k >= 3:
                    sy.wait_ge(sv, cast_done[k - 3])
                xf_ld[k] = stxf.issue(sy.dma_start(
                    out=xtf[:, k % 3, :], in_=x_ft[128 * k:128 * (k + 1), :]), k)
            @block.vector
            def _(v, k=k):
                stxf.wait(v, xf_ld[k])
                cast_done[k] = vop(v.tensor_copy(xT_bf[:, k, :],
                                                 xtf[:, k % 3, :]))
        def emit_wq_load(j):
            @block.gpsimd
            def _(g, j=j):
                if j >= 3:
                    g.wait_ge(stt, stripw_done[j - 3])
                wq_ld[j] = stwq.issue(g.dma_start(out=wqs[:, j % 3, :],
                                                  in_=wq_t[j]), j)
        for j in range(3):
            emit_wq_load(j)

        for k in range(NCH):
            @block.tensor
            def _(t, k=k):
                t.wait_ge(sv, cast_done[min(k + 2, NCH - 1)])
                if k == 0:
                    for j in range(3):
                        stwq.wait(t, wq_ld[j])
                for s in range(3):
                    lt = wqs[:, s, 128 * k:128 * (k + 1)]
                    top(t.matmul(pq[:, s, 0:512], lt, xT_bf[:, k, 0:512],
                                 start=(k == 0), stop=False))
                    top(t.matmul(pq[:, s, 512:1024], lt, xT_bf[:, k, 512:1024],
                                 start=(k == 0), stop=False))

        def strip_head(j):
            slot = j % 3
            @block.tensor
            def _(t, j=j, slot=slot):
                if j >= 2:
                    t.wait_ge(ss, wrow_done[j - 2])
                for k in range(NCH):
                    top(t.matmul(pwr1[:, j % 2, :], soncb[:, :],
                                 wqs[:, slot, 128 * k:128 * (k + 1)],
                                 start=(k == 0), stop=(k == NCH - 1)))
                stripw_done[j] = C["t"]
            @block.scalar
            def _(sc, j=j):
                sc.wait_ge(stt, stripw_done[j])
                if j >= 2:
                    sc.wait_ge(stt, rank1_done[j - 2])
                wrow_done[j] = sop(sc.copy(wrowb[:, j % 2, :],
                                           pwr1[:, j % 2, :]))
            @block.tensor
            def _(t, j=j, slot=slot):
                t.wait_ge(ss, wrow_done[j])
                top(t.matmul(pq[:, slot, 0:512], wrowb[:, j % 2, :],
                             negm_row[:, 0:512], start=False, stop=True))
                top(t.matmul(pq[:, slot, 512:1024], wrowb[:, j % 2, :],
                             negm_row[:, 512:1024], start=False, stop=True))
                rank1_done[j] = C["t"]

        def strip_elu(j):
            slot = j % 3
            @block.vector
            def _(v, j=j, slot=slot):
                v.wait_ge(stt, rank1_done[j])
                if j >= 1:
                    v.wait_ge(ss, ee_done[j - 1])
                    v.wait_ge(sv, sig_done[j - 1])
                qtmp_done[j] = vop(v.tensor_tensor(qtmp[:, :], pq[:, slot, :],
                                                   r_b[:, :], ALU.mult))
            @block.scalar
            def _(sc, j=j):
                sc.wait_ge(sv, qtmp_done[j])
                if j >= 1:
                    sc.wait_ge(sv, sig_done[j - 1])
                sop(sc.activation(er0[:, :], qtmp[:, :], AF.Relu,
                                  bias=sbq[:, j:j + 1]))
                sop(sc.activation(ern[:, :], qtmp[:, :], AF.Relu,
                                  bias=sbqn[:, j:j + 1], scale=neg1_col))
                sc.wait_ge(ss, C["s"])
                ee_done[j] = sop(sc.activation(qtmp[:, :], ern[:, :], AF.Exp,
                                               scale=neg1_col))
            @block.vector
            def _(v, j=j):
                v.wait_ge(ss, ee_done[j])
                sig_done[j] = vop(v.tensor_tensor(sigT[:, T * j:T * (j + 1)],
                                                  er0[:, :], qtmp[:, :], ALU.add))

        for j in range(3):
            strip_head(j)
            emit_wq_load(j + 3)
        strip_elu(0)
        NPRE = 6
        for j in range(3, NH):
            slot = j % 3
            @block.tensor
            def _(t, j=j, slot=slot):
                stwq.wait(t, wq_ld[j])
                t.wait_ge(sv, qtmp_done[j - 3])
                for k in range(NCH):
                    lt = wqs[:, slot, 128 * k:128 * (k + 1)]
                    top(t.matmul(pq[:, slot, 0:512], lt, xT_bf[:, k, 0:512],
                                 start=(k == 0), stop=False))
                    top(t.matmul(pq[:, slot, 512:1024], lt, xT_bf[:, k, 512:1024],
                                 start=(k == 0), stop=False))
            if j == NH - 1:
                bchunks_done = C["t"]
                es_xt.close()
                es_cw = ExitStack()
                wfs_r = es_cw.enter_context(nc.sbuf_tensor(
                    "wfs_r", [128, 2, D], BF16, side="right"))
                gs_pre = es_cw.enter_context(nc.sbuf_tensor(
                    "gs_pre", [128, NPRE, T], F32, side="right"))
                retrT = es_cw.enter_context(nc.sbuf_tensor(
                    "retrT", [128, NH * T], BF16, side="right"))
                es_p1 = ExitStack()
                pacc1 = es_p1.enter_context(nc.psum_tensor(
                    "pacc1", [128, 512], F32, side="right"))
            strip_head(j)
            strip_elu(j - 2)
            if j + 3 < NH:
                emit_wq_load(j + 3)
        strip_elu(NH - 2)
        strip_elu(NH - 1)

        es_bp.close()
        pacc1b = es_p1.enter_context(nc.psum_tensor(
            "pacc1b", [128, 512], F32, side="right"))

        wf_ld = [None] * NCH
        acc_done = [0] * NCH
        e1_done = [0] * NCH
        p1mm = [[0, 0] for _ in range(NPRE)]
        p1e1 = [[0, 0] for _ in range(NPRE)]

        KG = 4
        p1_list = [(c, half, kg) for c in range(NPRE)
                   for half in range(2) for kg in range(KG)]
        p1_pos = [0]

        P1SLOT = [2, 0, 1]

        def emit_p1_load(c):
            @block.gpsimd
            def _(g, c=c):
                if c < 3:
                    g.wait_ge(stt, stripw_done[5 + c])
                else:
                    g.wait_ge(stt, acc_done[c - 3])
                wf_ld[c] = stwf.issue(g.dma_start(
                    out=wqs[:, P1SLOT[c % 3], :], in_=wf_t[c]), c)

        def p1_drain(n):
            for _ in range(n):
                if p1_pos[0] >= len(p1_list):
                    return
                c, half, kg = p1_list[p1_pos[0]]
                p1_pos[0] += 1
                if half == 0 and kg == 0:
                    if c == 0:
                        emit_p1_load(0)
                    if c + 1 < NPRE:
                        emit_p1_load(c + 1)
                pb = pacc1 if half == 0 else pacc1b
                @block.tensor
                def _(t, c=c, half=half, kg=kg, pb=pb):
                    if half == 0 and kg == 0:
                        stwf.wait(t, wf_ld[c])
                    if kg == 0:
                        if c == 0 and half == 1:
                            t.wait_ge(ss, wrow_done[NH - 1])
                        if c:
                            t.wait_ge(sv, p1e1[c - 1][half])
                    for k in range(8 * kg, 8 * kg + 8):
                        lt = wqs[:, P1SLOT[c % 3], 128 * k:128 * (k + 1)]
                        base = T * k + 512 * half
                        top(t.matmul(pb[:, :], lt, xq_T[:, base:base + 512],
                                     start=(k == 0), stop=(k == NCH - 1)))
                    if kg == KG - 1:
                        p1mm[c][half] = C["t"]
                if kg == KG - 1:
                    @block.vector
                    def _(v, c=c, half=half, pb=pb):
                        v.wait_ge(stt, p1mm[c][half])
                        p1e1[c][half] = vop(v.tensor_tensor(
                            gs_pre[:, c, 512 * half:512 * (half + 1)],
                            pb[:, :], xs_b[:, 512 * half:512 * (half + 1)],
                            ALU.mult))
                    if half == 1:
                        acc_done[c] = p1mm[c][1]
                        e1_done[c] = p1e1[c][1]

        with (
            nc.sbuf_tensor("drow", [1, T], F32) as drow,
            nc.sbuf_tensor("drec", [1, 2, T], F32) as drec,
            nc.sbuf_tensor("db_sb", [128, 2, T], F32) as db_sb,
            nc.psum_tensor("pden", [1, T], F32) as pden,
            nc.psum_tensor("pdb", [128, T], F32) as pdb,
            nc.psum_tensor("pnum", [128, T], F32) as pnum,
        ):
            pden_done = [0] * NH
            drow_done = [0] * NH
            drec_done = [0] * NH
            pdb_done = [0] * NH
            db_evac = [0] * NH
            pnum_done = [0] * NH
            retr_done = [0] * NH

            def b2_head(h):
                @block.tensor
                def _(t, h=h):
                    t.wait_ge(sv, sig_done[h])
                    if h == 0:
                        t.wait_ge(sv, qtmp_done[NH - 1])
                        t.wait_ge(ss, wrow_done[NH - 1])
                        stx.wait_all(t)
                    if h >= 1:
                        t.wait_ge(sv, drow_done[h - 1])
                    sig = sigT[:, T * h:T * (h + 1)]
                    top(t.matmul(pden[:, 0:512], szT[:, h:h + 1],
                                 sig[:, 0:512], start=True, stop=True))
                    top(t.matmul(pden[:, 512:1024], szT[:, h:h + 1],
                                 sig[:, 512:1024], start=True, stop=True))
                    pden_done[h] = C["t"]
                p1_drain(3)
                @block.vector
                def _(v, h=h):
                    v.wait_ge(stt, pden_done[h])
                    if h >= 2:
                        v.wait_ge(stt, pdb_done[h - 2])
                    if h >= 1:
                        v.wait_ge(sv, drec_done[h - 1])
                    drow_done[h] = vop(v.tensor_scalar(
                        drow[:, :], pden[:, :], 1e-6, None, ALU.add))
                    v.wait_ge(sv, C["v"])
                    drec_done[h] = vop(v.reciprocal(drec[:, h % 2, :],
                                                    drow[:, :]))
                @block.tensor
                def _(t, h=h):
                    t.wait_ge(sv, drec_done[h])
                    if h >= 1:
                        t.wait_ge(ss, db_evac[h - 1])
                    top(t.matmul(pdb[:, 0:512], sonrf[:, :],
                                 drec[:, h % 2, 0:512], start=True, stop=True))
                    top(t.matmul(pdb[:, 512:1024], sonrf[:, :],
                                 drec[:, h % 2, 512:1024], start=True, stop=True))
                    pdb_done[h] = C["t"]
                    if h >= 1:
                        t.wait_ge(sv, retr_done[h - 1])
                    sig = sigT[:, T * h:T * (h + 1)]
                    top(t.matmul(pnum[:, 0:512],
                                 sm_t[:, 128 * h:128 * (h + 1)],
                                 sig[:, 0:512], start=True, stop=True))
                    top(t.matmul(pnum[:, 512:1024],
                                 sm_t[:, 128 * h:128 * (h + 1)],
                                 sig[:, 512:1024], start=True, stop=True))
                    pnum_done[h] = C["t"]
                p1_drain(2)
                @block.scalar
                def _(sc, h=h):
                    sc.wait_ge(stt, pdb_done[h])
                    if h >= 2:
                        sc.wait_ge(sv, retr_done[h - 2])
                    db_evac[h] = sop(sc.copy(db_sb[:, h % 2, :], pdb[:, :]))
                @block.vector
                def _(v, h=h):
                    v.wait_ge(stt, pnum_done[h])
                    v.wait_ge(ss, db_evac[h])
                    retr_done[h] = vop(v.tensor_tensor(
                        retrT[:, T * h:T * (h + 1)], pnum[:, :],
                        db_sb[:, h % 2, :], ALU.mult))

            p1_drain(6)
            for h in range(NH):
                b2_head(h)
            p1_drain(len(p1_list))
        es_p1.close()
        es_b.close()
        b2_v = retr_done[NH - 1]
        b2_s = db_evac[NH - 1]
        p1_v = p1e1[NPRE - 1][1]

        with (
            nc.sbuf_tensor("wos", [128, D], F32) as wos,
            nc.sbuf_tensor("eft1", [128, T], F32) as eft1,
            nc.sbuf_tensor("eft2", [128, T], F32) as eft2,
            nc.sbuf_tensor("effs", [128, 2, T], BF16) as effs,
            nc.sbuf_tensor("e1s", [128, 2, T], F32) as e1s,
            nc.sbuf_tensor("gs", [128, T], F32) as gs,
            nc.sbuf_tensor("sgs", [128, 2, T], F32) as sgs,
            nc.sbuf_tensor("xrs", [128, 2, T], F32) as xrs,
            nc.psum_tensor("pacc", [128, 2, T], F32) as pacc,
            nc.psum_tensor("pO", [128, 2, T], F32) as pO,
        ):
            eff_done = [0] * NCH
            o_done = [0] * NCH
            g_done = [0] * NCH
            po_read = [0] * NCH
            out_done = [0] * NCH
            wo_ld = [None] * NCH
            xr_ld = [None] * NCH
            st_done = [None] * NCH

            @block.scalar
            def _(sc):
                sc.preload_act_table(AF.Silu)

            for c in range(NCH):
                if c >= NPRE:
                    @block.gpsimd
                    def _(g, c=c):
                        g.wait_ge(stt, acc_done[c - 2])
                        wf_ld[c] = stwf.issue(g.dma_start(out=wfs_r[:, c % 2, :],
                                                          in_=wf_t[c]), c)
                @block.scalar
                def _(sc, c=c):
                    if c == 0:
                        sc.wait_ge(sv, b2_v)
                        sc.wait_ge(sv, p1_v)
                        sc.wait_ge(stt, acc_done[NPRE - 1])
                    if c >= 1:
                        sc.wait_ge(sv, eff_done[c - 1])
                    wo_ld[c] = stwo.issue(sc.dma_start(out=wos[:, :],
                                                       in_=wo_t[c]), c)
                @block.sync
                def _(sy, c=c):
                    if c == 0:
                        sy.wait_ge(sv, b2_v)
                        sy.wait_ge(ss, b2_s)
                        sy.wait_ge(sv, p1_v)
                        sy.wait_ge(stt, acc_done[NPRE - 1])
                    if c >= 2:
                        sy.wait_ge(sv, out_done[c - 2])
                    xr_ld[c] = stxr.issue(sy.dma_start(
                        out=xrs[:, c % 2, :],
                        in_=x_ft[128 * c:128 * (c + 1), :]), c)
                @block.vector
                def _(v, c=c):
                    stwo.wait(v, wo_ld[c])
                    if c >= 2:
                        v.wait_ge(stt, o_done[c - 2])
                    wv = wos[:, :].rearrange("p (b j) -> p b j", j=128)
                    vop(v.tensor_tensor(
                        eft1[:, :].rearrange("p (b j) -> p b j", j=128),
                        wv[:, 0:32:4, :], wv[:, 1:32:4, :], ALU.add))
                    vop(v.tensor_tensor(
                        eft2[:, :].rearrange("p (b j) -> p b j", j=128),
                        wv[:, 2:32:4, :], wv[:, 3:32:4, :], ALU.add))
                    v.wait_ge(sv, C["v"])
                    eff_done[c] = vop(v.tensor_tensor(effs[:, c % 2, :], eft1[:, :],
                                                      eft2[:, :], ALU.add))
                @block.tensor
                def _(t, c=c):
                    if c >= NPRE:
                        stwf.wait(t, wf_ld[c])
                        if c == NPRE:
                            t.wait_ge(sv, b2_v)
                            t.wait_ge(ss, b2_s)
                        if c >= NPRE + 2:
                            t.wait_ge(sv, e1_done[c - 2])
                        for k in range(NCH):
                            lt = wfs_r[:, c % 2, 128 * k:128 * (k + 1)]
                            top(t.matmul(pacc[:, c % 2, 0:512], lt,
                                         xq_T[:, T * k:T * k + 512],
                                         start=(k == 0), stop=(k == NCH - 1)))
                            top(t.matmul(pacc[:, c % 2, 512:1024], lt,
                                         xq_T[:, T * k + 512:T * (k + 1)],
                                         start=(k == 0), stop=(k == NCH - 1)))
                        acc_done[c] = C["t"]
                    t.wait_ge(sv, eff_done[c])
                    if c == 0:
                        t.wait_ge(sv, b2_v)
                        t.wait_ge(ss, b2_s)
                        t.wait_ge(sv, p1_v)
                    if c >= 2:
                        t.wait_ge(sv, po_read[c - 2])
                    for kh in range(NH):
                        lt = effs[:, c % 2, 128 * kh:128 * (kh + 1)]
                        top(t.matmul(pO[:, c % 2, 0:512], lt,
                                     retrT[:, T * kh:T * kh + 512],
                                     start=(kh == 0), stop=(kh == NH - 1)))
                        top(t.matmul(pO[:, c % 2, 512:1024], lt,
                                     retrT[:, T * kh + 512:T * (kh + 1)],
                                     start=(kh == 0), stop=(kh == NH - 1)))
                    o_done[c] = C["t"]
                if c >= NPRE:
                    @block.vector
                    def _(v, c=c):
                        v.wait_ge(stt, acc_done[c])
                        if c >= NPRE + 2:
                            v.wait_ge(ss, g_done[c - 2])
                        e1_done[c] = vop(v.tensor_tensor(e1s[:, c % 2, :],
                                                         pacc[:, c % 2, :],
                                                         xs_b[:, :], ALU.mult))
                @block.scalar
                def _(sc, c=c):
                    if c >= 1:
                        sc.wait_ge(sv, po_read[c - 1])
                    if c < NPRE:
                        if c == 0:
                            sc.wait_ge(sv, b2_v)
                            sc.wait_ge(stt, acc_done[NPRE - 1])
                        sc.wait_ge(sv, e1_done[c])
                        g_done[c] = sop(sc.activation(
                            gs[:, :], gs_pre[:, c, :], AF.Silu,
                            bias=sbi[:, c:c + 1], scale=sws[:, c:c + 1]))
                    else:
                        sc.wait_ge(sv, e1_done[c])
                        g_done[c] = sop(sc.activation(
                            gs[:, :], e1s[:, c % 2, :], AF.Silu,
                            bias=sbi[:, c:c + 1], scale=sws[:, c:c + 1]))
                @block.vector
                def _(v, c=c):
                    v.wait_ge(ss, g_done[c])
                    v.wait_ge(stt, o_done[c])
                    if c >= 2:
                        stout.wait(v, st_done[c - 2])
                    po_read[c] = vop(v.tensor_tensor(sgs[:, c % 2, :],
                                                     pO[:, c % 2, :],
                                                     gs[:, :], ALU.mult))
                    stxr.wait(v, xr_ld[c])
                    v.wait_ge(sv, C["v"])
                    out_done[c] = vop(v.tensor_tensor(sgs[:, c % 2, :],
                                                      sgs[:, c % 2, :],
                                                      xrs[:, c % 2, :], ALU.add))
                @block.sync
                def _(sy, c=c):
                    sy.wait_ge(sv, out_done[c])
                    st_done[c] = stout.issue(sy.dma_start(
                        out=out_T[128 * c:128 * (c + 1), :],
                        in_=sgs[:, c % 2, :]), c)

            @block.sync
            def _(sy):
                stout.wait_all(sy)
            @block.gpsimd
            def _(g):
                stout.wait_all(g)
        es_cw.close()
    return nc


_NC_CACHE = {}


def kernel(L0_attn_hidden_states, L0_attn_Wq, L0_attn_bq, L0_attn_Wk, L0_attn_bk,
           L0_attn_Wv, L0_attn_bv, L0_attn_Wo, L0_attn_M, L0_attn_z,
           L0_ffn_weight_q, L0_ffn_weight_scale, L0_ffn_bias):
    x = np.ascontiguousarray(np.asarray(L0_attn_hidden_states, np.float32)
                             .reshape(8192, 4096))
    xT = np.ascontiguousarray(x.T)
    Wq = np.asarray(L0_attn_Wq, np.float32)
    Wq_sel = np.ascontiguousarray(Wq.reshape(32, 128, 4096)[::4].reshape(1024, 4096))
    bq_sel = np.ascontiguousarray(
        np.asarray(L0_attn_bq, np.float32).reshape(32, 128)[::4].T)
    Wo = np.asarray(L0_attn_Wo, np.float32)
    Wf = np.asarray(L0_ffn_weight_q, np.int8)
    M = np.asarray(L0_attn_M, np.float32)
    z = np.asarray(L0_attn_z, np.float32)

    wq_t = _tile_lhsT(Wq_sel)
    wo_t = _tile_lhsT(Wo)
    wf_t = _tile_lhsT(Wf)
    m_t = np.ascontiguousarray(M.transpose(1, 0, 2).reshape(128, 1024))
    zT = np.ascontiguousarray(z.T)
    ws_c = np.ascontiguousarray(
        np.asarray(L0_ffn_weight_scale, np.float32).reshape(32, 128).T)
    bi_c = np.ascontiguousarray(
        np.asarray(L0_ffn_bias, np.float32).reshape(32, 128).T)
    consts = np.zeros((128, 4), np.float32)
    consts[:, 0] = 1e-5
    consts[:, 1] = -1.0
    consts[:, 2] = -257.0
    consts[:, 3] = 1e-6
    ident_b = np.eye(128, dtype=ml_dtypes.bfloat16)
    ident_f = np.eye(128, dtype=np.float32)
    ones_rf = np.ones((1, 128), np.float32)
    ones_cb = np.ones((128, 1), ml_dtypes.bfloat16)

    shared = dict(wq_t=wq_t, wo_t=wo_t, wf_t=wf_t, m_t=m_t, zT=zT, bq_c=bq_sel,
                  ws_c=ws_c, bi_c=bi_c, consts=consts, ident_b=ident_b,
                  ident_f=ident_f, ones_rf=ones_rf, ones_cb=ones_cb)
    in_maps = []
    for r in range(NCORES):
        m = dict(shared)
        m["x_rm"] = np.ascontiguousarray(x[1024 * r:1024 * (r + 1), :])
        m["x_ft"] = np.ascontiguousarray(xT[:, 1024 * r:1024 * (r + 1)])
        in_maps.append(m)

    if "nc" not in _NC_CACHE:
        _NC_CACHE["nc"] = build_nc()
    res = run_bass_kernel_spmd(_NC_CACHE["nc"], in_maps, core_ids=list(range(NCORES)))
    out_T = np.concatenate([res.results[r]["out_T"] for r in range(NCORES)], axis=1)
    return np.ascontiguousarray(out_T.T).reshape(2, 4096, 4096).astype(np.float32)


# version 19
# speedup vs baseline: 1.0382x; 1.0382x over previous
import numpy as np
import ml_dtypes
import concourse.bass as bass
import concourse.mybir as mybir
from concourse import dve_ops
from concourse.bass_utils import run_bass_kernel_spmd
from contextlib import ExitStack

F32 = mybir.dt.float32
BF16 = mybir.dt.bfloat16
I8 = mybir.dt.int8
AF = mybir.ActivationFunctionType
ALU = mybir.AluOpType
AX = mybir.AxisListType.X

NCORES = 8
T = 1024
D = 4096
NT = 8
NCH = 32
NH = 8
QG = (3, 3, 2)


def _tile_lhsT(W):
    O, I = W.shape
    t = np.ascontiguousarray(W.T).reshape(I // 128, 128, O // 128, 128)
    return np.ascontiguousarray(t.transpose(2, 1, 0, 3))


def build_nc():
    nc = bass.Bass("TRN2", target_bir_lowering=False)
    dr = lambda n, sh, dt, k="ExternalInput": nc.dram_tensor(n, sh, dt, kind=k).ap()
    x_rm = dr("x_rm", [T, D], F32)
    x_ft = dr("x_ft", [D, T], F32)
    wq_t = dr("wq_t", [NH, 128, NCH, 128], F32)
    wo_t = dr("wo_t", [NCH, 128, NCH, 128], F32)
    wf_t = dr("wf_t", [NCH, 128, NCH, 128], I8)
    m_t = dr("m_t", [128, NH * 128], F32)
    zT = dr("zT", [128, NH], F32)
    bq_c = dr("bq_c", [128, NH], F32)
    ws_c = dr("ws_c", [128, NCH], F32)
    bi_c = dr("bi_c", [128, NCH], F32)
    consts = dr("consts", [128, 4], F32)
    ident_b = dr("ident_b", [128, 128], BF16)
    ident_f = dr("ident_f", [128, 128], F32)
    ones_rf = dr("ones_rf", [1, 128], F32)
    ones_cb = dr("ones_cb", [128, 1], BF16)
    out_T = dr("out_T", [D, T], F32, "ExternalOutput")

    st = ExitStack()
    with st:
        sb = lambda n, sh, dt=F32: st.enter_context(nc.sbuf_tensor(n, sh, dt))
        xq_T = sb("xq_T", [128, NCH * T], BF16)
        cols = sb("cols", [128, 24])
        acols = sb("acols", [128, 16])
        mcols = sb("mcols", [128, 8])
        tcol = sb("tcol", [128, 8])
        sdcol = sb("sdcol", [128, 8])
        varp = sb("varp", [128, 8])
        invc = sb("invc", [128, 8])
        sm_t = sb("sm_t", [128, NH * 128], BF16)
        szT = sb("szT", [128, NH], BF16)
        sbq = sb("sbq", [128, NH])
        sbqn = sb("sbqn", [128, NH])
        sws = sb("sws", [128, NCH])
        sbi = sb("sbi", [128, NCH])
        scst = sb("scst", [128, 4])
        sidb = sb("sidb", [128, 128], BF16)
        sidf = sb("sidf", [128, 128], F32)
        sonrf = sb("sonrf", [1, 128], F32)
        soncb = sb("soncb", [128, 1], BF16)
        r_b = sb("r_b", [128, T])
        xs_b = sb("xs_b", [128, T])
        negm_row = sb("negm_row", [1, T], BF16)

        sx = st.enter_context(nc.semaphore("sx"))
        sg = st.enter_context(nc.semaphore("sg"))
        sv = st.enter_context(nc.semaphore("sv"))
        ss = st.enter_context(nc.semaphore("ss"))
        stt = st.enter_context(nc.semaphore("stt"))
        block = st.enter_context(nc.Block())

        C = {"x": 0, "g": 0, "v": 0, "s": 0, "t": 0}
        SEM = {"x": sx, "g": sg, "v": sv, "s": ss, "t": stt}

        def mk(tag, dma=False):
            d = 16 if dma else 1
            def f(inst):
                inst.then_inc(SEM[tag], d)
                C[tag] += d
                return C[tag]
            return f
        vop, sop, top = mk("v"), mk("s"), mk("t")
        xdma, gdma = mk("x", True), mk("g", True)

        class Stream:
            def __init__(self, name, n=2):
                self.sems = [st.enter_context(nc.semaphore(f"{name}{i}"))
                             for i in range(n)]
                self.cnt = [0] * n
                self.n = n
            def issue(self, inst, i):
                p = i % self.n
                inst.then_inc(self.sems[p], 16)
                self.cnt[p] += 16
                return (p, self.cnt[p])
            def wait(self, eng, tok):
                if tok:
                    eng.wait_ge(self.sems[tok[0]], tok[1])
            def wait_all(self, eng):
                for p in range(self.n):
                    if self.cnt[p]:
                        eng.wait_ge(self.sems[p], self.cnt[p])

        stx = Stream("stx")
        stxf = Stream("stxf", 3)
        stwq = Stream("stwq", 3)
        stwf = Stream("stwf", 3)
        stwo = Stream("stwo")
        stxr = Stream("stxr")
        stout = Stream("stout")
        ALL_STREAMS = [stx, stxf, stwq, stwf, stwo, stxr, stout]

        def full_barrier():
            marks = dict(C)
            for name in ("sync", "gpsimd", "vector", "scalar", "tensor"):
                sec = getattr(block, name)
                @sec
                def _(eng, marks=marks):
                    for tag in ("x", "g", "v", "s", "t"):
                        if marks[tag] > 0:
                            eng.wait_ge(SEM[tag], marks[tag])
                    for stream in ALL_STREAMS:
                        stream.wait_all(eng)

        @block.sync
        def _(sy):
            xdma(sy.dma_start(out=sbq[:, :], in_=bq_c))
            xdma(sy.dma_start(out=sws[:, :], in_=ws_c))
            xdma(sy.dma_start(out=sbi[:, :], in_=bi_c))
            xdma(sy.dma_start(out=scst[:, :], in_=consts))
            xdma(sy.dma_start(out=sidb[:, :], in_=ident_b))
            xdma(sy.dma_start(out=sidf[:, :], in_=ident_f))
            xdma(sy.dma_start(out=sonrf[:, :], in_=ones_rf))
            xdma(sy.dma_start(out=soncb[:, :], in_=ones_cb))
        p0_x = C["x"]
        @block.gpsimd
        def _(g):
            gdma(g.dma_start(out=sm_t[:, :], in_=m_t))
            gdma(g.dma_start(out=szT[:, :], in_=zT))
        @block.scalar
        def _(sc):
            sc.preload_act_table(AF.Sqrt)
        @block.vector
        def _(v):
            v.wait_ge(sx, p0_x)
            vop(v.tensor_scalar(sbqn[:, :], sbq[:, :], -1.0, None, ALU.mult))

        neg1_col = scst[:, 1:2]
        neg257_col = scst[:, 2:3]

        with (
            nc.sbuf_tensor("bx", [128, 2, D], F32) as bx,
            nc.sbuf_tensor("bxn", [128, D], F32) as bxn,
            nc.sbuf_tensor("scr", [128, D], F32) as scr,
            nc.sbuf_tensor("q0i", [128, D], mybir.dt.int32) as q0i,
            nc.sbuf_tensor("fm1", [128, D], F32) as fm1,
            nc.sbuf_tensor("bqbf", [128, 2, D], BF16) as bqbf,
            nc.sbuf_tensor("dums", [128, D], BF16) as dums,
            nc.sbuf_tensor("tp24", [24, 128], F32) as tp24,
            nc.sbuf_tensor("r_row", [1, T], F32) as r_row,
            nc.sbuf_tensor("xs_row", [1, T], F32) as xs_row,
            nc.psum_tensor("ptr", [128, 2, 1024], BF16) as ptr,
            nc.psum_tensor("pc24", [24, 128], F32) as pc24,
            nc.psum_tensor("pbc", [128, 1024], F32) as pbc,
        ):
            ld_done = [None] * NT
            sq_done = [0] * NT
            vcol_done = [0] * NT
            sqrt_done = [0] * NT
            xn_done = [0] * NT
            q0_done = [0] * NT
            fm1_done = [0] * NT
            e1_done = [0] * NT
            xq_done = [0] * NT
            trg = [[0] * 4 for _ in range(NT)]
            evg = [[0] * 4 for _ in range(NT)]

            def U1(tau):
                @block.sync
                def _(sy, tau=tau):
                    if tau >= 2:
                        sy.wait_ge(ss, sq_done[tau - 2])
                        sy.wait_ge(sv, xn_done[tau - 2])
                    ld_done[tau] = stx.issue(sy.dma_start(
                        out=bx[:, tau % 2, :],
                        in_=x_rm[128 * tau:128 * (tau + 1), :]), tau)

            def U2(tau):
                @block.scalar
                def _(sc, tau=tau):
                    stx.wait(sc, ld_done[tau])
                    sc.wait_ge(ss, C["s"])
                    sop(sc.activation(dums[:, :], bx[:, tau % 2, :], AF.Square,
                                      accum_out=acols[:, 2 * tau + 1:2 * tau + 2]))
                    sc.wait_ge(ss, C["s"])
                    sq_done[tau] = sop(sc.activation(
                        dums[:, :], bx[:, tau % 2, :], AF.Copy,
                        accum_out=acols[:, 2 * tau:2 * tau + 1]))

            def U3(tau):
                @block.vector
                def _(v, tau=tau):
                    a_sum = acols[:, 2 * tau:2 * tau + 1]
                    a_sq = acols[:, 2 * tau + 1:2 * tau + 2]
                    mc = mcols[:, tau:tau + 1]
                    v.wait_ge(ss, sq_done[tau])
                    vop(v.tensor_scalar(mc, a_sum, 1.0 / D, None, ALU.mult))
                    vop(v.tensor_scalar(tcol[:, 0:1], a_sq, 1.0 / D, None,
                                        ALU.mult))
                    v.wait_ge(sv, C["v"])
                    vop(v.tensor_tensor(tcol[:, 1:2], mc, mc, ALU.mult))
                    v.wait_ge(sv, C["v"])
                    vop(v.tensor_tensor(tcol[:, 2:3], tcol[:, 0:1], tcol[:, 1:2],
                                        ALU.subtract))
                    v.wait_ge(sv, C["v"])
                    vcol_done[tau] = vop(v.tensor_scalar(
                        varp[:, tau:tau + 1], tcol[:, 2:3], 1e-5, None, ALU.add))
                    vop(v.tensor_scalar(cols[:, tau:tau + 1], mc, -1.0, None,
                                        ALU.mult))

            def U4(tau):
                @block.scalar
                def _(sc, tau=tau):
                    sc.wait_ge(sv, vcol_done[tau])
                    sqrt_done[tau] = sop(sc.activation(
                        sdcol[:, tau:tau + 1], varp[:, tau:tau + 1], AF.Sqrt))

            def U5(tau):
                @block.vector
                def _(v, tau=tau):
                    rc = cols[:, 8 + tau:9 + tau]
                    vp = varp[:, tau:tau + 1]
                    v.wait_ge(ss, sqrt_done[tau])
                    vop(v.reciprocal(tcol[:, 3:4], sdcol[:, tau:tau + 1]))
                    v.wait_ge(sv, C["v"])
                    vop(v.tensor_tensor(tcol[:, 4:5], tcol[:, 3:4], tcol[:, 3:4],
                                        ALU.mult))
                    v.wait_ge(sv, C["v"])
                    vop(v.tensor_tensor(tcol[:, 5:6], tcol[:, 4:5], vp, ALU.mult))
                    v.wait_ge(sv, C["v"])
                    vop(v.tensor_scalar(tcol[:, 6:7], tcol[:, 5:6], -0.5, 1.5,
                                        ALU.mult, ALU.add))
                    v.wait_ge(sv, C["v"])
                    vop(v.tensor_tensor(rc, tcol[:, 3:4], tcol[:, 6:7], ALU.mult))
                    v.wait_ge(sv, C["v"])
                    xn_done[tau] = vop(v.tensor_scalar(
                        bxn[:, :], bx[:, tau % 2, :], mcols[:, tau:tau + 1], rc,
                        ALU.subtract, ALU.mult))
                    v.wait_ge(sv, C["v"])
                    vop(v.reduce_max(tcol[:, 7:8], bxn[:, :], axis=AX,
                                     apply_absolute_value=True))
                    v.wait_ge(sv, C["v"])
                    vop(v.tensor_scalar(cols[:, 16 + tau:17 + tau], tcol[:, 7:8],
                                        1e-8, float(1.0 / 127.0), ALU.max,
                                        ALU.mult))
                    v.wait_ge(sv, C["v"])
                    vop(v.reciprocal(invc[:, tau:tau + 1],
                                     cols[:, 16 + tau:17 + tau]))
                    v.wait_ge(sv, C["v"])
                    vop(v.tensor_scalar(scr[:, :], bxn[:, :],
                                        invc[:, tau:tau + 1], 256.5,
                                        ALU.mult, ALU.add))
                    v.wait_ge(sv, C["v"])
                    if tau >= 1:
                        v.wait_ge(ss, fm1_done[tau - 1])
                    q0_done[tau] = vop(v.tensor_copy(q0i[:, :], scr[:, :]))

            def U6(tau):
                @block.scalar
                def _(sc, tau=tau):
                    sc.wait_ge(sv, q0_done[tau])
                    if tau >= 1:
                        sc.wait_ge(sv, e1_done[tau - 1])
                    fm1_done[tau] = sop(sc.activation(fm1[:, :], q0i[:, :],
                                                      AF.Identity,
                                                      bias=neg257_col))

            def U7(tau):
                @block.vector
                def _(v, tau=tau):
                    v.wait_ge(ss, fm1_done[tau])
                    vop(v.scalar_tensor_tensor(
                        bxn[:, :], scr[:, :], -257.0, fm1[:, :], ALU.add,
                        ALU.subtract))
                    v.wait_ge(sv, C["v"])
                    e1_done[tau] = vop(v.scalar_tensor_tensor(
                        scr[:, :], bxn[:, :], 0.0, fm1[:, :], ALU.is_ge,
                        ALU.add))
                    v.wait_ge(sv, C["v"])
                    if tau >= 2:
                        v.wait_ge(stt, trg[tau - 2][3])
                    xq_done[tau] = vop(v.scalar_tensor_tensor(
                        bqbf[:, tau % 2, :], bxn[:, :], 1.0, scr[:, :],
                        ALU.is_ge, ALU.add))

            xqv = xq_T[:, :].rearrange("p (c t1) -> p c t1", t1=T)

            def U8(tau):
                for grp in range(4):
                    @block.tensor
                    def _(t, tau=tau, grp=grp):
                        if tau == 0 and grp == 0:
                            t.wait_ge(sx, p0_x)
                        t.wait_ge(sv, xq_done[tau])
                        prev = (evg[tau][grp - 2] if grp >= 2 else
                                (evg[tau - 1][grp + 2] if tau >= 1 else 0))
                        if prev:
                            t.wait_ge(ss, prev)
                        for j in range(8):
                            c = 8 * grp + j
                            top(t.transpose(
                                ptr[:, grp % 2, 128 * j:128 * (j + 1)],
                                bqbf[:, tau % 2, 128 * c:128 * (c + 1)],
                                sidb[:, :]))
                        trg[tau][grp] = C["t"]
                    @block.scalar
                    def _(sc, tau=tau, grp=grp):
                        sc.wait_ge(stt, trg[tau][grp])
                        evg[tau][grp] = sop(sc.copy(
                            xqv[:, 8 * grp:8 * grp + 8,
                                128 * tau:128 * (tau + 1)],
                            ptr[:, grp % 2, :].rearrange("p (j t1) -> p j t1",
                                                         t1=128)))

            U1(0); U2(0); U3(0); U4(0)
            U1(1); U2(1)
            U5(0); U6(0)
            U3(1); U4(1)
            U1(2); U2(2)
            for tau in range(NT):
                U7(tau)
                U8(tau)
                if tau + 1 < NT:
                    U5(tau + 1); U6(tau + 1)
                if tau + 2 < NT:
                    U3(tau + 2); U4(tau + 2)
                if tau + 3 < NT:
                    U1(tau + 3); U2(tau + 3)

            @block.tensor
            def _(t):
                t.wait_ge(sv, C["v"])
                top(t.transpose(pc24[:, :], cols[:, :], sidf[:, :]))
            @block.scalar
            def _(sc):
                sc.wait_ge(stt, C["t"])
                sop(sc.copy(tp24[:, :], pc24[:, :]))
            @block.gpsimd
            def _(g):
                g.wait_ge(ss, C["s"])
                gdma(g.dma_start(
                    out=negm_row[:, :].rearrange("o (b t1) -> o b t1", b=8),
                    in_=tp24[0:8, :]))
                gdma(g.dma_start(
                    out=r_row[:, :].rearrange("o (b t1) -> o b t1", b=8),
                    in_=tp24[8:16, :]))
                gdma(g.dma_start(
                    out=xs_row[:, :].rearrange("o (b t1) -> o b t1", b=8),
                    in_=tp24[16:24, :]))
            @block.tensor
            def _(t):
                t.wait_ge(sg, C["g"])
                top(t.matmul(pbc[:, 0:512], sonrf[:, :], r_row[:, 0:512],
                             start=True, stop=True))
                top(t.matmul(pbc[:, 512:1024], sonrf[:, :], r_row[:, 512:1024],
                             start=True, stop=True))
            @block.scalar
            def _(sc):
                sc.wait_ge(stt, C["t"])
                sop(sc.copy(r_b[:, :], pbc[:, :]))
            @block.tensor
            def _(t):
                t.wait_ge(ss, C["s"])
                top(t.matmul(pbc[:, 0:512], sonrf[:, :], xs_row[:, 0:512],
                             start=True, stop=True))
                top(t.matmul(pbc[:, 512:1024], sonrf[:, :], xs_row[:, 512:1024],
                             start=True, stop=True))
            @block.scalar
            def _(sc):
                sc.wait_ge(stt, C["t"])
                sop(sc.copy(xs_b[:, :], pbc[:, :]))

            full_barrier()

        sigT = st.enter_context(nc.sbuf_tensor("sigT", [128, NH * T], BF16))
        es_b = ExitStack()
        wqs = es_b.enter_context(nc.sbuf_tensor("wqs", [128, 3, D], BF16))
        qtmp = es_b.enter_context(nc.sbuf_tensor("qtmp", [128, T], F32))
        er0 = es_b.enter_context(nc.sbuf_tensor("er0", [128, T], F32))
        ern = es_b.enter_context(nc.sbuf_tensor("ern", [128, T], F32))
        wrowb = es_b.enter_context(nc.sbuf_tensor("wrowb", [1, 2, 128], BF16))
        es_bp = ExitStack()
        pq = es_bp.enter_context(nc.psum_tensor("pq", [128, 3, T], F32))
        pwr1 = es_bp.enter_context(nc.psum_tensor("pwr1", [1, 2, 128], F32))
        es_xt = ExitStack()
        xtf = es_xt.enter_context(nc.sbuf_tensor("xtf", [128, 3, T], F32))
        xT_bf = es_xt.enter_context(nc.sbuf_tensor("xT_bf", [128, NCH, T], BF16))

        xf_ld = [None] * NCH
        cast_done = [0] * NCH
        wq_ld = [None] * NH
        stripw_done = [0] * NH
        rank1_done = [0] * NH
        wrow_done = [0] * NH
        qtmp_done = [0] * NH
        ee_done = [0] * NH
        sig_done = [0] * NH

        @block.scalar
        def _(sc):
            sc.preload_act_table(AF.Exp)

        for k in range(NCH):
            @block.sync
            def _(sy, k=k):
                if k >= 3:
                    sy.wait_ge(sv, cast_done[k - 3])
                xf_ld[k] = stxf.issue(sy.dma_start(
                    out=xtf[:, k % 3, :], in_=x_ft[128 * k:128 * (k + 1), :]), k)
            @block.vector
            def _(v, k=k):
                stxf.wait(v, xf_ld[k])
                cast_done[k] = vop(v.tensor_copy(xT_bf[:, k, :],
                                                 xtf[:, k % 3, :]))
        def emit_wq_load(j):
            @block.gpsimd
            def _(g, j=j):
                if j >= 3:
                    g.wait_ge(stt, stripw_done[j - 3])
                wq_ld[j] = stwq.issue(g.dma_start(out=wqs[:, j % 3, :],
                                                  in_=wq_t[j]), j)
        for j in range(3):
            emit_wq_load(j)

        for k in range(NCH):
            @block.tensor
            def _(t, k=k):
                t.wait_ge(sv, cast_done[min(k + 2, NCH - 1)])
                if k == 0:
                    for j in range(3):
                        stwq.wait(t, wq_ld[j])
                for s in range(3):
                    lt = wqs[:, s, 128 * k:128 * (k + 1)]
                    top(t.matmul(pq[:, s, 0:512], lt, xT_bf[:, k, 0:512],
                                 start=(k == 0), stop=False))
                    top(t.matmul(pq[:, s, 512:1024], lt, xT_bf[:, k, 512:1024],
                                 start=(k == 0), stop=False))

        def strip_head(j):
            slot = j % 3
            @block.tensor
            def _(t, j=j, slot=slot):
                if j >= 2:
                    t.wait_ge(ss, wrow_done[j - 2])
                for k in range(NCH):
                    top(t.matmul(pwr1[:, j % 2, :], soncb[:, :],
                                 wqs[:, slot, 128 * k:128 * (k + 1)],
                                 start=(k == 0), stop=(k == NCH - 1)))
                stripw_done[j] = C["t"]
            @block.scalar
            def _(sc, j=j):
                sc.wait_ge(stt, stripw_done[j])
                if j >= 2:
                    sc.wait_ge(stt, rank1_done[j - 2])
                wrow_done[j] = sop(sc.copy(wrowb[:, j % 2, :],
                                           pwr1[:, j % 2, :]))
            @block.tensor
            def _(t, j=j, slot=slot):
                t.wait_ge(ss, wrow_done[j])
                top(t.matmul(pq[:, slot, 0:512], wrowb[:, j % 2, :],
                             negm_row[:, 0:512], start=False, stop=True))
                top(t.matmul(pq[:, slot, 512:1024], wrowb[:, j % 2, :],
                             negm_row[:, 512:1024], start=False, stop=True))
                rank1_done[j] = C["t"]

        def strip_elu(j):
            slot = j % 3
            @block.vector
            def _(v, j=j, slot=slot):
                v.wait_ge(stt, rank1_done[j])
                if j >= 1:
                    v.wait_ge(ss, ee_done[j - 1])
                    v.wait_ge(sv, sig_done[j - 1])
                qtmp_done[j] = vop(v.tensor_tensor(qtmp[:, :], pq[:, slot, :],
                                                   r_b[:, :], ALU.mult))
            @block.scalar
            def _(sc, j=j):
                sc.wait_ge(sv, qtmp_done[j])
                if j >= 1:
                    sc.wait_ge(sv, sig_done[j - 1])
                sop(sc.activation(er0[:, :], qtmp[:, :], AF.Relu,
                                  bias=sbq[:, j:j + 1]))
                sop(sc.activation(ern[:, :], qtmp[:, :], AF.Relu,
                                  bias=sbqn[:, j:j + 1], scale=neg1_col))
                sc.wait_ge(ss, C["s"])
                ee_done[j] = sop(sc.activation(qtmp[:, :], ern[:, :], AF.Exp,
                                               scale=neg1_col))
            @block.vector
            def _(v, j=j):
                v.wait_ge(ss, ee_done[j])
                sig_done[j] = vop(v.tensor_tensor(sigT[:, T * j:T * (j + 1)],
                                                  er0[:, :], qtmp[:, :], ALU.add))

        for j in range(3):
            strip_head(j)
            emit_wq_load(j + 3)
        strip_elu(0)
        NPRE = 6
        for j in range(3, NH):
            slot = j % 3
            @block.tensor
            def _(t, j=j, slot=slot):
                stwq.wait(t, wq_ld[j])
                t.wait_ge(sv, qtmp_done[j - 3])
                for k in range(NCH):
                    lt = wqs[:, slot, 128 * k:128 * (k + 1)]
                    top(t.matmul(pq[:, slot, 0:512], lt, xT_bf[:, k, 0:512],
                                 start=(k == 0), stop=False))
                    top(t.matmul(pq[:, slot, 512:1024], lt, xT_bf[:, k, 512:1024],
                                 start=(k == 0), stop=False))
            if j == NH - 1:
                bchunks_done = C["t"]
                es_xt.close()
                es_cw = ExitStack()
                wfs_r = es_cw.enter_context(nc.sbuf_tensor(
                    "wfs_r", [128, 2, D], BF16, side="right"))
                gs_pre = es_cw.enter_context(nc.sbuf_tensor(
                    "gs_pre", [128, NPRE, T], F32, side="right"))
                retrT = es_cw.enter_context(nc.sbuf_tensor(
                    "retrT", [128, NH * T], BF16, side="right"))
                es_p1 = ExitStack()
                pacc1 = es_p1.enter_context(nc.psum_tensor(
                    "pacc1", [128, 512], F32, side="right"))
            strip_head(j)
            strip_elu(j - 2)
            if j + 3 < NH:
                emit_wq_load(j + 3)
        strip_elu(NH - 2)
        strip_elu(NH - 1)

        es_bp.close()
        pacc1b = es_p1.enter_context(nc.psum_tensor(
            "pacc1b", [128, 512], F32, side="right"))

        wf_ld = [None] * NCH
        acc_done = [0] * NCH
        e1_done = [0] * NCH
        p1mm = [[0, 0] for _ in range(NPRE)]
        p1e1 = [[0, 0] for _ in range(NPRE)]

        KG = 4
        p1_list = [(c, half, kg) for c in range(NPRE)
                   for half in range(2) for kg in range(KG)]
        p1_pos = [0]

        P1SLOT = [2, 0, 1]

        def emit_p1_load(c):
            @block.gpsimd
            def _(g, c=c):
                if c < 3:
                    g.wait_ge(stt, stripw_done[5 + c])
                else:
                    g.wait_ge(stt, acc_done[c - 3])
                wf_ld[c] = stwf.issue(g.dma_start(
                    out=wqs[:, P1SLOT[c % 3], :], in_=wf_t[c]), c)

        def p1_drain(n):
            for _ in range(n):
                if p1_pos[0] >= len(p1_list):
                    return
                c, half, kg = p1_list[p1_pos[0]]
                p1_pos[0] += 1
                if half == 0 and kg == 0:
                    if c == 0:
                        emit_p1_load(0)
                    if c + 1 < NPRE:
                        emit_p1_load(c + 1)
                pb = pacc1 if half == 0 else pacc1b
                @block.tensor
                def _(t, c=c, half=half, kg=kg, pb=pb):
                    if half == 0 and kg == 0:
                        stwf.wait(t, wf_ld[c])
                    if kg == 0:
                        if c == 0 and half == 1:
                            t.wait_ge(ss, wrow_done[NH - 1])
                        if c:
                            t.wait_ge(sv, p1e1[c - 1][half])
                    for k in range(8 * kg, 8 * kg + 8):
                        lt = wqs[:, P1SLOT[c % 3], 128 * k:128 * (k + 1)]
                        base = T * k + 512 * half
                        top(t.matmul(pb[:, :], lt, xq_T[:, base:base + 512],
                                     start=(k == 0), stop=(k == NCH - 1)))
                    if kg == KG - 1:
                        p1mm[c][half] = C["t"]
                if kg == KG - 1:
                    @block.vector
                    def _(v, c=c, half=half, pb=pb):
                        v.wait_ge(stt, p1mm[c][half])
                        p1e1[c][half] = vop(v.tensor_tensor(
                            gs_pre[:, c, 512 * half:512 * (half + 1)],
                            pb[:, :], xs_b[:, 512 * half:512 * (half + 1)],
                            ALU.mult))
                    if half == 1:
                        acc_done[c] = p1mm[c][1]
                        e1_done[c] = p1e1[c][1]

        with (
            nc.sbuf_tensor("denr", [1, 2, T], F32) as denr,
            nc.sbuf_tensor("dinvT", [128, 2, 64], F32) as dinvT,
            nc.sbuf_tensor("drec_r", [1, 2, T], F32) as drec_r,
            nc.sbuf_tensor("db_sb", [128, T], F32) as db_sb,
            nc.psum_tensor("pden", [1, T], F32) as pden,
            nc.psum_tensor("pdb", [128, T], F32) as pdb,
            nc.psum_tensor("pnum", [128, T], F32) as pnum,
        ):
            pden_done = [0] * NH
            den_ev = [0] * NH
            pdb_done = [0] * NH
            db_evac = [0] * NH
            pnum_done = [0] * NH
            retr_done = [0] * NH

            tcol_done = [0] * NH
            for h in range(NH):
                @block.tensor
                def _(t, h=h):
                    t.wait_ge(sv, sig_done[h])
                    if h == 0:
                        t.wait_ge(sv, qtmp_done[NH - 1])
                        t.wait_ge(ss, wrow_done[NH - 1])
                        stx.wait_all(t)
                    if h >= 1:
                        t.wait_ge(ss, den_ev[h - 1])
                    sig = sigT[:, T * h:T * (h + 1)]
                    top(t.matmul(pden[:, 0:512], szT[:, h:h + 1],
                                 sig[:, 0:512], start=True, stop=True))
                    top(t.matmul(pden[:, 512:1024], szT[:, h:h + 1],
                                 sig[:, 512:1024], start=True, stop=True))
                    pden_done[h] = C["t"]
                p1_drain(1)
                @block.scalar
                def _(sc, h=h):
                    sc.wait_ge(stt, pden_done[h])
                    if h >= 2:
                        sc.wait_ge(stt, tcol_done[h - 2])
                    den_ev[h] = sop(sc.copy(denr[:, h % 2, :], pden[:, :]))
                @block.tensor
                def _(t, h=h):
                    t.wait_ge(ss, den_ev[h])
                    for k8 in range(8):
                        top(t.transpose(pdb[:, 8 * k8 + h:8 * k8 + h + 1],
                                        denr[0:1, h % 2,
                                             128 * k8:128 * (k8 + 1)],
                                        sidf[0:1, 0:1]))
                    tcol_done[h] = C["t"]
                p1_drain(1)
            tin_done = tcol_done[NH - 1]
            @block.vector
            def _(v):
                v.wait_ge(stt, tin_done)
                vop(v.tensor_scalar(dinvT[:, 0, :], pdb[:, 0:64], 1e-6, None,
                                    ALU.add))
                v.wait_ge(sv, C["v"])
                vop(v.reciprocal(dinvT[:, 1, :], dinvT[:, 0, :]))
            dinv_done = C["v"]
            dcp = [0] * NH
            tb_done = [0] * NH
            for h in range(NH):
                @block.tensor
                def _(t, h=h):
                    t.wait_ge(sv, dinv_done)
                    if h >= 1:
                        t.wait_ge(ss, dcp[h - 1])
                    for k8 in range(8):
                        top(t.transpose(pden[:, 128 * k8:128 * (k8 + 1)],
                                        dinvT[:, 1, 8 * k8 + h:8 * k8 + h + 1],
                                        sidf[:, :]))
                    tb_done[h] = C["t"]
                @block.scalar
                def _(sc, h=h):
                    sc.wait_ge(stt, tb_done[h])
                    if h >= 2:
                        sc.wait_ge(stt, pdb_done[h - 2])
                    dcp[h] = sop(sc.copy(drec_r[:, h % 2, :], pden[:, :]))
                @block.tensor
                def _(t, h=h):
                    t.wait_ge(ss, dcp[h])
                    if h >= 1:
                        t.wait_ge(ss, db_evac[h - 1])
                    top(t.matmul(pdb[:, 0:512], sonrf[:, :],
                                 drec_r[:, h % 2, 0:512], start=True, stop=True))
                    top(t.matmul(pdb[:, 512:1024], sonrf[:, :],
                                 drec_r[:, h % 2, 512:1024],
                                 start=True, stop=True))
                    pdb_done[h] = C["t"]
                    if h >= 1:
                        t.wait_ge(sv, retr_done[h - 1])
                    sig = sigT[:, T * h:T * (h + 1)]
                    top(t.matmul(pnum[:, 0:512],
                                 sm_t[:, 128 * h:128 * (h + 1)],
                                 sig[:, 0:512], start=True, stop=True))
                    top(t.matmul(pnum[:, 512:1024],
                                 sm_t[:, 128 * h:128 * (h + 1)],
                                 sig[:, 512:1024], start=True, stop=True))
                    pnum_done[h] = C["t"]
                p1_drain(3)
                @block.scalar
                def _(sc, h=h):
                    sc.wait_ge(stt, pdb_done[h])
                    if h >= 1:
                        sc.wait_ge(sv, retr_done[h - 1])
                    db_evac[h] = sop(sc.copy(db_sb[:, :], pdb[:, :]))
                @block.vector
                def _(v, h=h):
                    v.wait_ge(stt, pnum_done[h])
                    v.wait_ge(ss, db_evac[h])
                    retr_done[h] = vop(v.tensor_tensor(
                        retrT[:, T * h:T * (h + 1)], pnum[:, :],
                        db_sb[:, :], ALU.mult))

            p1_drain(len(p1_list))
        es_p1.close()
        es_b.close()
        b2_v = retr_done[NH - 1]
        b2_s = db_evac[NH - 1]
        p1_v = p1e1[NPRE - 1][1]

        with (
            nc.sbuf_tensor("wos", [128, D], F32) as wos,
            nc.sbuf_tensor("eft1", [128, T], F32) as eft1,
            nc.sbuf_tensor("eft2", [128, T], F32) as eft2,
            nc.sbuf_tensor("effs", [128, 2, T], BF16) as effs,
            nc.sbuf_tensor("e1s", [128, 2, T], F32) as e1s,
            nc.sbuf_tensor("gs", [128, T], F32) as gs,
            nc.sbuf_tensor("sgs", [128, 2, T], F32) as sgs,
            nc.sbuf_tensor("xrs", [128, 2, T], F32) as xrs,
            nc.psum_tensor("pacc", [128, 2, T], F32) as pacc,
            nc.psum_tensor("pO", [128, 2, T], F32) as pO,
        ):
            eff_done = [0] * NCH
            o_done = [0] * NCH
            g_done = [0] * NCH
            po_read = [0] * NCH
            out_done = [0] * NCH
            wo_ld = [None] * NCH
            xr_ld = [None] * NCH
            st_done = [None] * NCH

            @block.scalar
            def _(sc):
                sc.preload_act_table(AF.Silu)

            for c in range(NCH):
                if c >= NPRE:
                    @block.gpsimd
                    def _(g, c=c):
                        g.wait_ge(stt, acc_done[c - 2])
                        wf_ld[c] = stwf.issue(g.dma_start(out=wfs_r[:, c % 2, :],
                                                          in_=wf_t[c]), c)
                @block.scalar
                def _(sc, c=c):
                    if c == 0:
                        sc.wait_ge(sv, b2_v)
                        sc.wait_ge(sv, p1_v)
                        sc.wait_ge(stt, acc_done[NPRE - 1])
                    if c >= 1:
                        sc.wait_ge(sv, eff_done[c - 1])
                    wo_ld[c] = stwo.issue(sc.dma_start(out=wos[:, :],
                                                       in_=wo_t[c]), c)
                @block.sync
                def _(sy, c=c):
                    if c == 0:
                        sy.wait_ge(sv, b2_v)
                        sy.wait_ge(ss, b2_s)
                        sy.wait_ge(sv, p1_v)
                        sy.wait_ge(stt, acc_done[NPRE - 1])
                    if c >= 2:
                        sy.wait_ge(sv, out_done[c - 2])
                    xr_ld[c] = stxr.issue(sy.dma_start(
                        out=xrs[:, c % 2, :],
                        in_=x_ft[128 * c:128 * (c + 1), :]), c)
                @block.vector
                def _(v, c=c):
                    stwo.wait(v, wo_ld[c])
                    if c >= 2:
                        v.wait_ge(stt, o_done[c - 2])
                    wv = wos[:, :].rearrange("p (b j) -> p b j", j=128)
                    vop(v.tensor_tensor(
                        eft1[:, :].rearrange("p (b j) -> p b j", j=128),
                        wv[:, 0:32:4, :], wv[:, 1:32:4, :], ALU.add))
                    vop(v.tensor_tensor(
                        eft2[:, :].rearrange("p (b j) -> p b j", j=128),
                        wv[:, 2:32:4, :], wv[:, 3:32:4, :], ALU.add))
                    v.wait_ge(sv, C["v"])
                    eff_done[c] = vop(v.tensor_tensor(effs[:, c % 2, :], eft1[:, :],
                                                      eft2[:, :], ALU.add))
                @block.tensor
                def _(t, c=c):
                    if c >= NPRE:
                        stwf.wait(t, wf_ld[c])
                        if c == NPRE:
                            t.wait_ge(sv, b2_v)
                            t.wait_ge(ss, b2_s)
                        if c >= NPRE + 2:
                            t.wait_ge(sv, e1_done[c - 2])
                        for k in range(NCH):
                            lt = wfs_r[:, c % 2, 128 * k:128 * (k + 1)]
                            top(t.matmul(pacc[:, c % 2, 0:512], lt,
                                         xq_T[:, T * k:T * k + 512],
                                         start=(k == 0), stop=(k == NCH - 1)))
                            top(t.matmul(pacc[:, c % 2, 512:1024], lt,
                                         xq_T[:, T * k + 512:T * (k + 1)],
                                         start=(k == 0), stop=(k == NCH - 1)))
                        acc_done[c] = C["t"]
                    t.wait_ge(sv, eff_done[c])
                    if c == 0:
                        t.wait_ge(sv, b2_v)
                        t.wait_ge(ss, b2_s)
                        t.wait_ge(sv, p1_v)
                    if c >= 2:
                        t.wait_ge(sv, po_read[c - 2])
                    for kh in range(NH):
                        lt = effs[:, c % 2, 128 * kh:128 * (kh + 1)]
                        top(t.matmul(pO[:, c % 2, 0:512], lt,
                                     retrT[:, T * kh:T * kh + 512],
                                     start=(kh == 0), stop=(kh == NH - 1)))
                        top(t.matmul(pO[:, c % 2, 512:1024], lt,
                                     retrT[:, T * kh + 512:T * (kh + 1)],
                                     start=(kh == 0), stop=(kh == NH - 1)))
                    o_done[c] = C["t"]
                if c >= NPRE:
                    @block.vector
                    def _(v, c=c):
                        v.wait_ge(stt, acc_done[c])
                        if c >= NPRE + 2:
                            v.wait_ge(ss, g_done[c - 2])
                        e1_done[c] = vop(v.tensor_tensor(e1s[:, c % 2, :],
                                                         pacc[:, c % 2, :],
                                                         xs_b[:, :], ALU.mult))
                @block.scalar
                def _(sc, c=c):
                    if c >= 1:
                        sc.wait_ge(sv, po_read[c - 1])
                    if c < NPRE:
                        if c == 0:
                            sc.wait_ge(sv, b2_v)
                            sc.wait_ge(stt, acc_done[NPRE - 1])
                        sc.wait_ge(sv, e1_done[c])
                        g_done[c] = sop(sc.activation(
                            gs[:, :], gs_pre[:, c, :], AF.Silu,
                            bias=sbi[:, c:c + 1], scale=sws[:, c:c + 1]))
                    else:
                        sc.wait_ge(sv, e1_done[c])
                        g_done[c] = sop(sc.activation(
                            gs[:, :], e1s[:, c % 2, :], AF.Silu,
                            bias=sbi[:, c:c + 1], scale=sws[:, c:c + 1]))
                @block.vector
                def _(v, c=c):
                    v.wait_ge(ss, g_done[c])
                    v.wait_ge(stt, o_done[c])
                    if c >= 2:
                        stout.wait(v, st_done[c - 2])
                    po_read[c] = vop(v.tensor_tensor(sgs[:, c % 2, :],
                                                     pO[:, c % 2, :],
                                                     gs[:, :], ALU.mult))
                    stxr.wait(v, xr_ld[c])
                    v.wait_ge(sv, C["v"])
                    out_done[c] = vop(v.tensor_tensor(sgs[:, c % 2, :],
                                                      sgs[:, c % 2, :],
                                                      xrs[:, c % 2, :], ALU.add))
                @block.sync
                def _(sy, c=c):
                    sy.wait_ge(sv, out_done[c])
                    st_done[c] = stout.issue(sy.dma_start(
                        out=out_T[128 * c:128 * (c + 1), :],
                        in_=sgs[:, c % 2, :]), c)

            @block.sync
            def _(sy):
                stout.wait_all(sy)
            @block.gpsimd
            def _(g):
                stout.wait_all(g)
        es_cw.close()
    return nc


_NC_CACHE = {}


def kernel(L0_attn_hidden_states, L0_attn_Wq, L0_attn_bq, L0_attn_Wk, L0_attn_bk,
           L0_attn_Wv, L0_attn_bv, L0_attn_Wo, L0_attn_M, L0_attn_z,
           L0_ffn_weight_q, L0_ffn_weight_scale, L0_ffn_bias):
    x = np.ascontiguousarray(np.asarray(L0_attn_hidden_states, np.float32)
                             .reshape(8192, 4096))
    xT = np.ascontiguousarray(x.T)
    Wq = np.asarray(L0_attn_Wq, np.float32)
    Wq_sel = np.ascontiguousarray(Wq.reshape(32, 128, 4096)[::4].reshape(1024, 4096))
    bq_sel = np.ascontiguousarray(
        np.asarray(L0_attn_bq, np.float32).reshape(32, 128)[::4].T)
    Wo = np.asarray(L0_attn_Wo, np.float32)
    Wf = np.asarray(L0_ffn_weight_q, np.int8)
    M = np.asarray(L0_attn_M, np.float32)
    z = np.asarray(L0_attn_z, np.float32)

    wq_t = _tile_lhsT(Wq_sel)
    wo_t = _tile_lhsT(Wo)
    wf_t = _tile_lhsT(Wf)
    m_t = np.ascontiguousarray(M.transpose(1, 0, 2).reshape(128, 1024))
    zT = np.ascontiguousarray(z.T)
    ws_c = np.ascontiguousarray(
        np.asarray(L0_ffn_weight_scale, np.float32).reshape(32, 128).T)
    bi_c = np.ascontiguousarray(
        np.asarray(L0_ffn_bias, np.float32).reshape(32, 128).T)
    consts = np.zeros((128, 4), np.float32)
    consts[:, 0] = 1e-5
    consts[:, 1] = -1.0
    consts[:, 2] = -257.0
    consts[:, 3] = 1e-6
    ident_b = np.eye(128, dtype=ml_dtypes.bfloat16)
    ident_f = np.eye(128, dtype=np.float32)
    ones_rf = np.ones((1, 128), np.float32)
    ones_cb = np.ones((128, 1), ml_dtypes.bfloat16)

    shared = dict(wq_t=wq_t, wo_t=wo_t, wf_t=wf_t, m_t=m_t, zT=zT, bq_c=bq_sel,
                  ws_c=ws_c, bi_c=bi_c, consts=consts, ident_b=ident_b,
                  ident_f=ident_f, ones_rf=ones_rf, ones_cb=ones_cb)
    in_maps = []
    for r in range(NCORES):
        m = dict(shared)
        m["x_rm"] = np.ascontiguousarray(x[1024 * r:1024 * (r + 1), :])
        m["x_ft"] = np.ascontiguousarray(xT[:, 1024 * r:1024 * (r + 1)])
        in_maps.append(m)

    if "nc" not in _NC_CACHE:
        _NC_CACHE["nc"] = build_nc()
    res = run_bass_kernel_spmd(_NC_CACHE["nc"], in_maps, core_ids=list(range(NCORES)))
    out_T = np.concatenate([res.results[r]["out_T"] for r in range(NCORES)], axis=1)
    return np.ascontiguousarray(out_T.T).reshape(2, 4096, 4096).astype(np.float32)
